# version 07
# baseline (speedup 1.0000x reference)
import numpy as np
import ml_dtypes
from contextlib import ExitStack
from collections import defaultdict

import concourse.bass as bass
import concourse.mybir as mybir
from concourse.bass_utils import run_bass_kernel_spmd

F32 = mybir.dt.float32
BF16 = mybir.dt.bfloat16
I32 = mybir.dt.int32
AF = mybir.ActivationFunctionType
ALU = mybir.AluOpType
AX = mybir.AxisListType

B = 2
T = 4096
D = 2048
NKV = 8
HD = 64
FF = 5632
CT = 512
NTILE = T // CT
DC = D // 128
FFC = FF // 128
OWN = 2 * CT

C_BSH, C_SCL, C_EPS, C_NH = 0, 1, 2, 3

_cache = {}


def _slot_perm():
    perm = [0] * 32
    for i in range(NKV):
        for r in range(4):
            perm[8 * (i // 2) + (i % 2) + 2 * r] = 4 * i + r
    return perm


def _host_prep(L0_attn_x, L0_attn_Wq, L0_attn_Wk, L0_attn_Wv, L0_attn_Wo,
               L0_ffn_w1, L0_ffn_w2, L0_ffn_w3):
    f32 = np.float32
    x = np.asarray(L0_attn_x, f32)
    Wq = np.asarray(L0_attn_Wq, f32)
    Wk = np.asarray(L0_attn_Wk, f32)
    Wv = np.asarray(L0_attn_Wv, f32)
    Wo = np.asarray(L0_attn_Wo, f32)
    w1 = np.asarray(L0_ffn_w1, f32)
    w2 = np.asarray(L0_ffn_w2, f32)
    w3 = np.asarray(L0_ffn_w3, f32)

    perm = _slot_perm()
    colperm = np.concatenate([np.arange(HD) + HD * qh for qh in perm])
    Wq_p = np.ascontiguousarray(Wq[:, colperm])
    Wo_p = np.ascontiguousarray(Wo[colperm, :])

    x_fm = np.ascontiguousarray(x.transpose(0, 2, 1))
    w1t = np.ascontiguousarray(w1.T)
    w2t = np.ascontiguousarray(w2.T)
    w3t = np.ascontiguousarray(w3.T)

    j = np.arange(0, HD, 2, dtype=f32) / HD
    theta = (10000.0 ** -j).astype(f32)
    ang = np.outer(np.arange(T, dtype=f32), theta)
    ropec = np.tile(np.cos(ang).T, (4, 1)).astype(f32)
    ropes = np.tile(np.sin(ang).T, (4, 1)).astype(f32)

    ki = np.arange(128)[:, None]
    qj = np.arange(CT)[None, :]
    masks = np.concatenate(
        [(128 * s + ki <= qj).astype(f32) for s in range(4)], axis=1)

    consts = np.zeros((128, 8), f32)
    consts[:, C_BSH] = -16.0
    consts[:, C_SCL] = 0.125
    consts[:, C_EPS] = 1e-5
    consts[:, C_NH] = -0.5

    in_maps = []
    for c in range(8):
        b, jj = c // 4, c % 4
        ca, cb = jj, 7 - jj
        xq = np.concatenate([x_fm[b][:, CT * ca:CT * (ca + 1)],
                             x_fm[b][:, CT * cb:CT * (cb + 1)]], axis=1)
        rqc = np.concatenate([ropec[:, CT * ca:CT * (ca + 1)],
                              ropec[:, CT * cb:CT * (cb + 1)]], axis=1)
        rqs = np.concatenate([ropes[:, CT * ca:CT * (ca + 1)],
                              ropes[:, CT * cb:CT * (cb + 1)]], axis=1)
        in_maps.append(dict(
            x=np.ascontiguousarray(x_fm[b]), xq=np.ascontiguousarray(xq),
            wq=Wq_p, wk=Wk, wv=Wv, wo=Wo_p, w1t=w1t, w2t=w2t, w3t=w3t,
            ropec=ropec, ropes=ropes,
            ropeqc=np.ascontiguousarray(rqc), ropeqs=np.ascontiguousarray(rqs),
            masks=masks, consts=consts, pat=np.array([[jj]], np.int32)))
    return in_maps


def _gamma_plan():
    plan = []
    for name in ("w1t", "w2t"):
        for rc in range(DC):
            for c0, ncol in ((0, 2048), (2048, 2048), (4096, 1536)):
                plan.append((name, 128 * rc, c0, ncol))
    for rc in range(FFC):
        plan.append(("w3t", 128 * rc, 0, 2048))
    return plan


class Carve:
    def __init__(self, pool, off, n, name):
        self.pool, self.off, self.n, self.name = pool, off, n, name

    def __getitem__(self, idx):
        rs, cs = idx
        c0 = cs.start or 0
        c1 = cs.stop if cs.stop is not None else self.n
        assert 0 <= c0 <= c1 <= self.n, f"{self.name}[{c0}:{c1}] n={self.n}"
        return self.pool.t[rs, self.off + c0:self.off + c1]

    def ap(self, coloff, tail, row0=0, nrow=128):
        return bass.AP(self.pool.t, row0 * self.pool.W + self.off + coloff,
                       [[self.pool.W, nrow]] + tail)


class Pool:
    def __init__(self, t, width):
        self.t, self.W = t, width

    def at(self, off, n, name):
        assert off + n <= self.W, f"{name}: {off}+{n} > {self.W}"
        return Carve(self, off, n, name)


PBW = 82176
PFW = 4608


class _StopBuild(Exception):
    pass


def build_program(stop_after=None, skip=()):
    nc = bass.Bass("TRN2", target_bir_lowering=False,
                   detect_race_conditions=False)

    x_d = nc.dram_tensor("x", [D, T], F32, kind="ExternalInput")
    xq_d = nc.dram_tensor("xq", [D, OWN], F32, kind="ExternalInput")
    wq_d = nc.dram_tensor("wq", [D, D], F32, kind="ExternalInput")
    wk_d = nc.dram_tensor("wk", [D, 512], F32, kind="ExternalInput")
    wv_d = nc.dram_tensor("wv", [D, 512], F32, kind="ExternalInput")
    wo_d = nc.dram_tensor("wo", [D, D], F32, kind="ExternalInput")
    w1t_d = nc.dram_tensor("w1t", [D, FF], F32, kind="ExternalInput")
    w2t_d = nc.dram_tensor("w2t", [D, FF], F32, kind="ExternalInput")
    w3t_d = nc.dram_tensor("w3t", [FF, D], F32, kind="ExternalInput")
    ropec_d = nc.dram_tensor("ropec", [128, T], F32, kind="ExternalInput")
    ropes_d = nc.dram_tensor("ropes", [128, T], F32, kind="ExternalInput")
    ropeqc_d = nc.dram_tensor("ropeqc", [128, OWN], F32, kind="ExternalInput")
    ropeqs_d = nc.dram_tensor("ropeqs", [128, OWN], F32, kind="ExternalInput")
    masks_d = nc.dram_tensor("masks", [128, 4 * CT], F32, kind="ExternalInput")
    consts_d = nc.dram_tensor("consts", [128, 8], F32, kind="ExternalInput")
    pat_d = nc.dram_tensor("pat", [1, 1], I32, kind="ExternalInput")
    out_d = nc.dram_tensor("out", [D, OWN], F32, kind="ExternalOutput")
    gsc_d = nc.dram_tensor("gsc", [1, 8], F32)
    x1s_d = nc.dram_tensor("x1s", [128, DC * OWN], BF16)
    dbg_d = nc.dram_tensor("dbg", [128, 2048], BF16, kind="ExternalOutput")
    dbgf_d = nc.dram_tensor("dbgf", [128, 16], F32, kind="ExternalOutput")

    gplan = _gamma_plan()
    NG = len(gplan)

    root = ExitStack()
    try:
      with root:
        pb = Pool(root.enter_context(nc.sbuf_tensor("poolb", [128, PBW], BF16)), PBW)
        pf = Pool(root.enter_context(nc.sbuf_tensor("poolf", [128, PFW], F32)), PFW)

        def SB(n, shape, dt):
            return root.enter_context(nc.sbuf_tensor(n, shape, dt))

        cst = SB("cst", [128, 8], F32)
        thr = SB("thr", [128, 8], F32)
        gacc = SB("gacc", [128, NG], F32)
        gcb = SB("gcb", [128, NG], BF16)
        grow = SB("grow", [1, 16], F32)
        ones_c = SB("ones_c", [128, 1], BF16)
        ones_r = SB("ones_r", [1, 128], BF16)
        patb = SB("patb", [1, 1], I32)
        rowm = SB("rowm", [1, CT], F32)
        rowv = SB("rowv", [1, CT], F32)
        rowr = SB("rowr", [1, CT], F32)
        rowmb = SB("rowmb", [1, CT], BF16)
        rowrb = SB("rowrb", [1, CT], BF16)
        rownb = SB("rownb", [1, CT], BF16)
        mbb = SB("mbb", [128, CT], BF16)
        rbb = SB("rbb", [128, CT], BF16)
        nbb = SB("nbb", [128, CT], BF16)
        kro = SB("kro", [128, CT], BF16)
        rtmp = SB("rtmp", [128, CT], BF16)
        cf = SB("cf", [128, CT], BF16)
        ctmp = SB("ctmp", [64, CT], BF16)
        recb = SB("recb", [1, 2 * CT], BF16)
        probs = [SB(f"probs{i}", [128, 2 * CT], BF16) for i in range(2)]
        sigb = SB("sigb", [128, 2 * CT], BF16)
        aosq = SB("aosq", [128, CT], BF16)
        fsq = SB("fsq", [128, 2 * CT], BF16)

        sems = {}
        for n in ("cst", "pat", "vone", "msk", "rope", "wkv", "x", "sq", "stat",
                  "row1", "rowsc", "rowb", "bc", "bce", "h", "kvp", "kvr",
                  "g", "gf", "ga", "gb",
                  "xq", "wq", "qp", "qr", "hq",
                  "st", "exp", "mk", "pv", "rec", "ct", "cd", "cb", "cm",
                  "wo", "ao", "aoe", "x1", "h2", "xr",
                  "w12", "w12q", "ffm", "sig", "hid", "xs",
                  "w3", "w3q", "w3m", "fe", "fst", "fin", "out"):
            sems[n] = root.enter_context(nc.semaphore(n))

        bodies = defaultdict(list)

        def on(eng, fn):
            bodies[eng].append(fn)

        def flush(name):
            nonlocal bodies
            with nc.Block(name) as blk:
                for eng in ("sync", "gpsimd", "tensor", "scalar", "vector"):
                    fns = bodies.get(eng)
                    if fns:
                        getattr(blk, eng)(lambda e, fns=fns: [f(e) for f in fns])
            bodies = defaultdict(list)
            if stop_after == name:
                raise _StopBuild()

        def W(e, n, v):
            e.wait_ge(sems[n], v)

        def cc(col):
            return cst[:, col:col + 1]

        def cc1(col):
            return cst[0:1, col:col + 1]

        def bc_ap(t, nrep):
            return bass.AP(t, 0, [[CT, 128], [0, nrep], [1, CT]])

        def ln_rows(idx, sa, sq2, mbp, rbp, stat_at, want_n):
            def v1(vv):
                W(vv, "stat", stat_at)
                W(vv, "bc", idx)
                vv.tensor_single_scalar(rowm[0:1, :], sa[0:1, :], 1.0 / D, ALU.mult)
                vv.tensor_tensor(rowv[0:1, :], rowm[0:1, :], rowm[0:1, :], ALU.mult)
                vv.scalar_tensor_tensor(rowv[0:1, :], sq2[0:1, :], 1.0 / D,
                                        rowv[0:1, :], ALU.mult, ALU.subtract).then_inc(sems["row1"], 1)
            on("vector", v1)

            def s1(sc):
                W(sc, "row1", idx + 1)
                sc.activation(rowv[0:1, :], rowv[0:1, :], AF.Ln, bias=cc1(C_EPS))
                sc.activation(rowr[0:1, :], rowv[0:1, :], AF.Exp, scale=cc1(C_NH)
                              ).then_inc(sems["rowsc"], 1)
            on("scalar", s1)

            def v2(vv):
                W(vv, "rowsc", idx + 1)
                vv.tensor_copy(rowrb[0:1, :], rowr[0:1, :])
                if want_n:
                    vv.tensor_tensor(rowm[0:1, :], rowm[0:1, :], rowr[0:1, :],
                                     ALU.mult)
                    vv.tensor_single_scalar(rowm[0:1, :], rowm[0:1, :], -1.0,
                                            ALU.mult)
                    vv.tensor_copy(rownb[0:1, :], rowm[0:1, :])
                else:
                    vv.tensor_copy(rowmb[0:1, :], rowm[0:1, :]).then_inc(sems["rowb"], 1)
            on("vector", v2)

            def tb(t):
                W(t, "rowb", idx + 1)
                W(t, "bce", idx)
                t.matmul(rbp[:, :], ones_r[0:1, :], rowrb[0:1, :],
                         start=True, stop=True)
                t.matmul(mbp[:, :], ones_r[0:1, :],
                         (rownb if want_n else rowmb)[0:1, :],
                         start=True, stop=True).then_inc(sems["bc"], 1)
            on("tensor", tb)

            def ve(vv):
                W(vv, "bc", idx + 1)
                vv.tensor_copy(rbb[:, :], rbp[:, :])
                vv.tensor_copy((nbb if want_n else mbb)[:, :], mbp[:, :]).then_inc(sems["bce"], 1)
            on("vector", ve)

        K_sb = pb.at(0, 4 * T, "K")
        V_sb = pb.at(16384, NTILE * 4 * 520, "V")
        mk_sb = pb.at(33024, 4 * CT, "mk")
        xt0 = pb.at(35072, DC * CT, "xt")
        xt = [xt0, xt0]
        ht = pb.at(43264, DC * CT, "ht")
        xsq = ht
        wkb = pb.at(51456, DC * 512, "wkb")
        wvb = pb.at(59648, DC * 512, "wvb")
        rcC = pb.at(67840, T, "rcC")
        rcS = pb.at(71936, T, "rcS")
        gbuf = [pf.at(i * 2048, 2048, f"gbuf{i}") for i in range(2)]

        psA = ExitStack()
        st1 = psA.enter_context(nc.psum_tensor("st1", [1, CT], F32))
        st2 = psA.enter_context(nc.psum_tensor("st2", [1, CT], F32))
        mbp = psA.enter_context(nc.psum_tensor("mbp", [128, CT], F32))
        rbp = psA.enter_context(nc.psum_tensor("rbp", [128, CT], F32))
        kvp = [psA.enter_context(nc.psum_tensor(f"kvp{i}", [128, CT], F32))
               for i in range(2)]

        def g_loads(g):
            g.dma_start(mk_sb[:, 0:4 * CT], masks_d[:, :]).then_inc(sems["msk"], 16)
            g.dma_start(rcC[:, 0:T], ropec_d[:, :]).then_inc(sems["rope"], 16)
            g.dma_start(rcS[:, 0:T], ropes_d[:, :]).then_inc(sems["rope"], 16)
            iap = bass.AP(wk_d, 0, [[512, 128], [128 * 512, DC], [1, 512]])
            g.dma_start(wkb.ap(0, [[512, DC], [1, 512]]), iap).then_inc(sems["wkv"], 16)
            iap = bass.AP(wv_d, 0, [[512, 128], [128 * 512, DC], [1, 512]])
            g.dma_start(wvb.ap(0, [[512, DC], [1, 512]]), iap).then_inc(sems["wkv"], 16)
        on("gpsimd", g_loads)

        def s_loads(s):
            s.dma_start(cst[:, :], consts_d[:, :]).then_inc(sems["cst"], 16)
            s.dma_start(patb[:, :], pat_d[:, :]).then_inc(sems["pat"], 16)
        on("sync", s_loads)

        def v_ones(vv):
            vv.memset(ones_c[:, :], 1.0)
            vv.memset(ones_r[:, :], 1.0)
            vv.memset(V_sb.ap(64, [[520, NTILE * 4], [65, 8], [1, 1]]), 1.0).then_inc(sems["vone"], 1)
        on("vector", v_ones)

        gdma = [0]
        gred = [0]

        def emit_gamma_dma(n):
            if "nogamma1" in skip:
                return
            for _ in range(n):
                i = gdma[0]
                if i >= NG:
                    return
                gdma[0] += 1
                name, r0, c0, ncol = gplan[i]
                dram = {"w1t": w1t_d, "w2t": w2t_d, "w3t": w3t_d}[name]
                fcols = FF if name != "w3t" else D

                def fn(s, i=i, dram=dram, r0=r0, c0=c0, ncol=ncol, fcols=fcols):
                    if i >= 2:
                        W(s, "gf", i - 1)
                    iap = bass.AP(dram, r0 * fcols + c0, [[fcols, 128], [1, ncol]])
                    s.dma_start(gbuf[i % 2][:, 0:ncol], iap).then_inc(sems["g"], 16)
                on("sync", fn)

        def emit_gamma_red(n):
            if "nogamma1" in skip:
                return
            for _ in range(n):
                i = gred[0]
                if i >= NG:
                    return
                gred[0] += 1
                _, _, _, ncol = gplan[i]

                def fn(vv, i=i, ncol=ncol):
                    W(vv, "g", 16 * (i + 1))
                    vv.tensor_reduce(gacc[:, i:i + 1], gbuf[i % 2][:, 0:ncol],
                                     AX.X, ALU.add, apply_absolute_value=True).then_inc(sems["gf"], 1)
                on("vector", fn)

        _nt = [f for f in skip if f.startswith("ntl")]
        _nt = int(_nt[0][3:]) if _nt else NTILE
        for tt in range(_nt):
            def g(g, tt=tt):
                if tt >= 1:
                    W(g, "h", tt)
                    W(g, "sq", tt)
                iap = bass.AP(x_d, CT * tt, [[T, 128], [128 * T, DC], [1, CT]])
                g.dma_start(xt[tt % 2].ap(0, [[CT, DC], [1, CT]]), iap
                            ).then_inc(sems["x"], 16)
            on("gpsimd", g)

            def sc(sc, tt=tt):
                W(sc, "x", 16 * (tt + 1))
                if tt >= 1:
                    W(sc, "stat", tt)
                    W(sc, "kvp", 8 * tt)
                sc.activation(xsq[:, 0:DC * CT], xt[tt % 2][:, 0:DC * CT],
                              AF.Square).then_inc(sems["sq"], 1)
            on("scalar", sc)

            def te(t, tt=tt):
                W(t, "sq", tt + 1)
                if tt == 0:
                    W(t, "vone", 1)
                    W(t, "cst", 16)
                W(t, "row1", tt)
                for c in range(DC):
                    t.matmul(st1[0:1, :], ones_c[:, :],
                             xt[tt % 2][:, CT * c:CT * (c + 1)],
                             start=(c == 0), stop=(c == DC - 1))
                for c in range(DC):
                    mm = t.matmul(st2[0:1, :], ones_c[:, :],
                                  xsq[:, CT * c:CT * (c + 1)],
                                  start=(c == 0), stop=(c == DC - 1))
                mm.then_inc(sems["stat"], 1)
            on("tensor", te)

            ln_rows(tt, st1, st2, mbp, rbp, stat_at=tt + 1, want_n=False)

            def vh(vv, tt=tt):
                W(vv, "bce", tt + 1)
                if tt >= 1:
                    W(vv, "kvp", 8 * tt)
                src = xt[tt % 2].ap(0, [[CT, DC], [1, CT]])
                dst = ht.ap(0, [[CT, DC], [1, CT]])
                vv.tensor_tensor(dst, src, bc_ap(mbb, DC), ALU.subtract)
                vv.tensor_tensor(dst, dst, bc_ap(rbb, DC), ALU.mult).then_inc(sems["h"], 1)
            on("vector", vh)

            for gi in range(8):
                isV = gi >= 4
                oc = gi % 4
                gg = 8 * tt + gi

                def tep(t, gi=gi, isV=isV, oc=oc, tt=tt, gg=gg):
                    if gi == 0:
                        W(t, "h", tt + 1)
                        if tt == 0:
                            W(t, "wkv", 32)
                    if gg >= 2:
                        W(t, "kvr", gg - 1)
                    if not isV:
                        for c in range(DC):
                            mm = t.matmul(
                                kvp[gg % 2][:, :],
                                wkb[:, 512 * c + 128 * oc:512 * c + 128 * (oc + 1)],
                                ht[:, CT * c:CT * (c + 1)],
                                start=(c == 0), stop=(c == DC - 1))
                    else:
                        for c in range(DC):
                            mm = t.matmul(
                                kvp[gg % 2][:, :],
                                ht[:, CT * c + 128 * oc:CT * c + 128 * (oc + 1)],
                                wvb[:, 512 * c:512 * (c + 1)],
                                start=(c == 0), stop=(c == DC - 1))
                    mm.then_inc(sems["kvp"], 1)
                on("tensor", tep)

                if not isV:
                    def vr(vv, oc=oc, tt=tt, gg=gg):
                        W(vv, "kvp", gg + 1)
                        if gg == 0:
                            W(vv, "rope", 32)
                        vv.tensor_copy(kro[:, :], kvp[gg % 2][:, :])
                        base = T * oc + CT * tt
                        c0 = CT * tt
                        for hb in (0, 64):
                            k1 = kro[hb:hb + 32, :]
                            k2 = kro[hb + 32:hb + 64, :]
                            o1 = K_sb[hb:hb + 32, base:base + CT]
                            o2 = K_sb[hb + 32:hb + 64, base:base + CT]
                            t1 = rtmp[hb:hb + 32, :]
                            t2 = rtmp[hb + 32:hb + 64, :]
                            vv.tensor_tensor(o1, k1, rcC[hb:hb + 32, c0:c0 + CT], ALU.mult)
                            vv.tensor_tensor(o2, k2, rcC[hb + 32:hb + 64, c0:c0 + CT], ALU.mult)
                            vv.tensor_tensor(t1, k2, rcS[hb + 32:hb + 64, c0:c0 + CT], ALU.mult)
                            vv.tensor_tensor(t2, k1, rcS[hb:hb + 32, c0:c0 + CT], ALU.mult)
                            vv.tensor_tensor(o1, o1, t1, ALU.subtract)
                            last = vv.tensor_tensor(o2, o2, t2, ALU.add)
                        last.then_inc(sems["kvr"], 1)
                    on("vector", vr)
                else:
                    def vev(vv, oc=oc, tt=tt, gg=gg):
                        W(vv, "kvp", gg + 1)
                        tc = 4 * tt + oc
                        dst = V_sb.ap(520 * tc, [[65, 8], [1, 64]])
                        src = bass.AP(kvp[gg % 2], 0, [[512, 128], [64, 8], [1, 64]])
                        vv.tensor_copy(dst, src).then_inc(sems["kvr"], 1)
                    on("vector", vev)

            emit_gamma_dma(18)
            emit_gamma_red(18)

        emit_gamma_dma(NG - gdma[0])
        emit_gamma_red(NG - gred[0])
        flush("blk1")

        Q_sb = pb.at(43264, DC * OWN, "Q")
        hq = pb.at(59648, DC * OWN, "hq")
        xqt = pb.at(35072, DC * CT, "xqt")
        xsq2 = pb.at(76032, 4096, "xsq2")
        rqC = pb.at(80128, OWN, "rqC")
        rqS = pb.at(81152, OWN, "rqS")
        wqb = [pb.at(76032 + i * 2048, DC * 128, f"wqb{i}") for i in range(2)]

        def g2(g):
            if "noblk2" in skip:
                return
            g.dma_start(rqC[:, 0:OWN], ropeqc_d[:, :]).then_inc(sems["rope"], 16)
            g.dma_start(rqS[:, 0:OWN], ropeqs_d[:, :]).then_inc(sems["rope"], 16)
            for tl in range(2):
                if tl == 1:
                    W(g, "hq", 1)
                    W(g, "sq", NTILE + 1)
                iap = bass.AP(xq_d, CT * tl, [[OWN, 128], [128 * OWN, DC], [1, CT]])
                g.dma_start(xqt.ap(0, [[CT, DC], [1, CT]]), iap
                            ).then_inc(sems["xq"], 16)
            W(g, "hq", 2)
            for o in range(DC):
                if o >= 2:
                    W(g, "qp", 2 * (o - 1))
                iap = bass.AP(wq_d, 128 * o, [[D, 128], [128 * D, DC], [1, 128]])
                g.dma_start(wqb[o % 2].ap(0, [[128, DC], [1, 128]]), iap
                            ).then_inc(sems["wq"], 16)
        on("gpsimd", g2)

        for tl in (() if "noblk2" in skip else range(2)):
            idx = NTILE + tl

            for half in range(2):
                def sc(sc, tl=tl, idx=idx, half=half):
                    W(sc, "xq", 16 * (tl + 1))
                    if not (tl == 0 and half == 0):
                        W(sc, "stat", 2 * idx - NTILE + half)
                    sc.activation(xsq2[:, 0:4096],
                                  xqt[:, 4096 * half:4096 * (half + 1)],
                                  AF.Square).then_inc(sems["sq"], 1)
                on("scalar", sc)

                def te(t, tl=tl, idx=idx, half=half):
                    W(t, "sq", NTILE + 2 * tl + half + 1)
                    if half == 0:
                        W(t, "row1", idx)
                    for c in range(8):
                        t.matmul(st1[0:1, :], ones_c[:, :],
                                 xqt[:, 4096 * half + CT * c:4096 * half + CT * (c + 1)],
                                 start=(half == 0 and c == 0),
                                 stop=(half == 1 and c == 7))
                    for c in range(8):
                        mm = t.matmul(st2[0:1, :], ones_c[:, :],
                                      xsq2[:, CT * c:CT * (c + 1)],
                                      start=(half == 0 and c == 0),
                                      stop=(half == 1 and c == 7))
                    mm.then_inc(sems["stat"], 1)
                on("tensor", te)

            ln_rows(idx, st1, st2, mbp, rbp, stat_at=NTILE + 2 * (tl + 1), want_n=False)

            def vh(vv, tl=tl, idx=idx):
                W(vv, "bce", idx + 1)
                src = xqt.ap(0, [[CT, DC], [1, CT]])
                dst = hq.ap(CT * tl, [[OWN, DC], [1, CT]])
                vv.tensor_tensor(dst, src, bc_ap(mbb, DC), ALU.subtract)
                vv.tensor_tensor(dst, dst, bc_ap(rbb, DC), ALU.mult).then_inc(sems["hq"], 1)
            on("vector", vh)

        for o in (() if "noblk2" in skip else range(DC)):
            for tl in range(2):
                gi = 2 * o + tl

                def te(t, o=o, tl=tl, gi=gi):
                    if gi == 0:
                        W(t, "hq", 2)
                    W(t, "wq", 16 * (o + 1))
                    if gi >= 2:
                        W(t, "qr", gi - 1)
                    for c in range(DC):
                        mm = t.matmul(
                            kvp[gi % 2][:, :], wqb[o % 2][:, 128 * c:128 * (c + 1)],
                            hq[:, OWN * c + CT * tl:OWN * c + CT * (tl + 1)],
                            start=(c == 0), stop=(c == DC - 1))
                    mm.then_inc(sems["qp"], 1)
                on("tensor", te)

                def vr(vv, o=o, tl=tl, gi=gi):
                    W(vv, "qp", gi + 1)
                    if gi == 0:
                        W(vv, "rope", 64)
                    vv.tensor_copy(kro[:, :], kvp[gi % 2][:, :])
                    base = OWN * o + CT * tl
                    c0 = CT * tl
                    for hb in (0, 64):
                        k1 = kro[hb:hb + 32, :]
                        k2 = kro[hb + 32:hb + 64, :]
                        o1 = Q_sb[hb:hb + 32, base:base + CT]
                        o2 = Q_sb[hb + 32:hb + 64, base:base + CT]
                        t1 = rtmp[hb:hb + 32, :]
                        t2 = rtmp[hb + 32:hb + 64, :]
                        vv.tensor_tensor(o1, k1, rqC[hb:hb + 32, c0:c0 + CT], ALU.mult)
                        vv.tensor_tensor(o2, k2, rqC[hb + 32:hb + 64, c0:c0 + CT], ALU.mult)
                        vv.tensor_tensor(t1, k2, rqS[hb + 32:hb + 64, c0:c0 + CT], ALU.mult)
                        vv.tensor_tensor(t2, k1, rqS[hb:hb + 32, c0:c0 + CT], ALU.mult)
                        vv.tensor_tensor(o1, o1, t1, ALU.subtract)
                        last = vv.tensor_tensor(o2, o2, t2, ALU.add)
                    last.then_inc(sems["qr"], 1)
                on("vector", vr)

        flush("blk2")
        psA.close()

        ctx_sb = pb.at(59648, DC * OWN, "ctx")

        psB = ExitStack()
        st_ps = [psB.enter_context(nc.psum_tensor(f"stp{i}", [128, 2 * CT], F32))
                 for i in range(2)]
        ctxe_ps = psB.enter_context(nc.psum_tensor("ctxe", [128, CT], F32))
        ctxo_ps = psB.enter_context(nc.psum_tensor("ctxo", [128, CT], F32))
        bc_ps = psB.enter_context(nc.psum_tensor("bcp", [128, CT], F32))

        def plan_attention(pat):
            fns = defaultdict(list)
            cnt = defaultdict(int)

            def lon(eng, fn):
                fns[eng].append(fn)

            gix = [0]
            for ci, cg in enumerate((pat, 7 - pat)):
                nkb = 4 * (cg + 1)
                for k in range(16):
                    cell = 16 * ci + k
                    ktile = k // 4
                    hv = 2 * (k // 4)
                    pv_queue = []

                    def flush_pv(n):
                        while len(pv_queue) > n:
                            lon("tensor", pv_queue.pop(0))

                    for kb in range(nkb):
                        gi = gix[0]
                        gix[0] += 1
                        par = gi % 2
                        isdiag = kb >= nkb - 4
                        sloc = kb - (nkb - 4)
                        kc = T * ktile + 128 * kb
                        qc = OWN * k + CT * ci
                        vc = 520 * kb + 65 * hv

                        def te_s(t, gi=gi, par=par, kc=kc, qc=qc,
                                 wexp=cnt["exp"] - 1):
                            if gi >= 2:
                                W(t, "exp", wexp)
                            t.matmul(st_ps[par][:, 0:CT], K_sb[0:64, kc:kc + 128],
                                     Q_sb[0:64, qc:qc + CT],
                                     start=True, stop=True, tile_position=(0, 0))
                            t.matmul(st_ps[par][:, CT:2 * CT],
                                     K_sb[64:128, kc:kc + 128],
                                     Q_sb[64:128, qc:qc + CT],
                                     start=True, stop=True,
                                     tile_position=(64, 0)).then_inc(sems["st"], 1)
                        lon("tensor", te_s)
                        cnt["st"] += 1

                        def sc_e(sc, gi=gi, par=par, wst=cnt["st"],
                                 wpv=cnt["pv"] - 1):
                            W(sc, "st", wst)
                            if gi >= 2:
                                W(sc, "pv", wpv)
                            sc.activation(probs[par][:, :], st_ps[par][:, :],
                                          AF.Exp, bias=cc(C_BSH), scale=cc(C_SCL)
                                          ).then_inc(sems["exp"], 1)
                        lon("scalar", sc_e)
                        cnt["exp"] += 1

                        if isdiag:
                            def v_m(vv, par=par, sloc=sloc, wexp=cnt["exp"]):
                                W(vv, "exp", wexp)
                                m = mk_sb.ap(CT * sloc, [[0, 2], [1, CT]])
                                p = bass.AP(probs[par], 0,
                                            [[2 * CT, 128], [CT, 2], [1, CT]])
                                vv.tensor_tensor(p, p, m, ALU.mult).then_inc(sems["mk"], 1)
                            lon("vector", v_m)
                            cnt["mk"] += 1

                        def te_pv(t, par=par, vc=vc, isdiag=isdiag, cell=cell,
                                  first=(kb == 0), last=(kb == nkb - 1),
                                  wmk=cnt["mk"], wexp=cnt["exp"],
                                  wrec=cnt["rec"], wct=cnt["ct"]):
                            if isdiag:
                                W(t, "mk", wmk)
                            else:
                                W(t, "exp", wexp)
                            if first and cell >= 1:
                                W(t, "rec", wrec)
                                W(t, "ct", wct)
                            t.matmul(ctxe_ps[0:65, :], V_sb[:, vc:vc + 65],
                                     probs[par][:, 0:CT], start=first, stop=last)
                            t.matmul(ctxo_ps[0:65, :], V_sb[:, vc + 65:vc + 130],
                                     probs[par][:, CT:2 * CT],
                                     start=first, stop=last
                                     ).then_inc(sems["pv"], 1)
                        pv_queue.append(te_pv)
                        flush_pv(1)
                        cnt["pv"] += 1
                    flush_pv(0)

                    def v_rec(vv, wpv=cnt["pv"]):
                        W(vv, "pv", wpv)
                        vv.reciprocal(rowm[0:1, :], ctxe_ps[64:65, :])
                        vv.reciprocal(rowv[0:1, :], ctxo_ps[64:65, :])
                        vv.tensor_copy(recb[0:1, 0:CT], rowm[0:1, :])
                        vv.tensor_copy(recb[0:1, CT:2 * CT], rowv[0:1, :]).then_inc(sems["rec"], 1)
                    lon("vector", v_rec)
                    cnt["rec"] += 1

                    def sc_cf(sc, wpv=cnt["pv"], wcm=cnt["cm"], wcd=cnt["cd"],
                              wrec=cnt["rec"]):
                        W(sc, "pv", wpv)
                        W(sc, "rec", wrec)
                        W(sc, "cm", wcm)
                        W(sc, "cd", 16 * wcd)
                        sc.copy(cf[0:64, :], ctxe_ps[0:64, :])
                        sc.copy(ctmp[0:64, :], ctxo_ps[0:64, :]
                                ).then_inc(sems["ct"], 1)
                    lon("scalar", sc_cf)
                    cnt["ct"] += 1

                    def te_bc(t, wrec=cnt["rec"], wcm=cnt["cm"]):
                        W(t, "rec", wrec)
                        W(t, "cm", wcm)
                        t.matmul(bc_ps[0:64, :], ones_r[0:1, 0:64],
                                 recb[0:1, 0:CT], start=True, stop=True,
                                 tile_position=(0, 0))
                        t.matmul(bc_ps[64:128, :], ones_r[0:1, 0:64],
                                 recb[0:1, CT:2 * CT], start=True, stop=True,
                                 tile_position=(0, 64)).then_inc(sems["cb"], 1)
                    lon("tensor", te_bc)
                    cnt["cb"] += 1
                    cnt["cd"] += 1

                    def v_mul(vv, wcb=cnt["cb"], wcd=cnt["cd"], wct=cnt["ct"],
                              k=k, ci=ci):
                        W(vv, "cb", wcb)
                        W(vv, "cd", 16 * wcd)
                        W(vv, "ct", wct)
                        col = OWN * k + CT * ci
                        vv.tensor_tensor(ctx_sb[:, col:col + CT], cf[:, :],
                                         bc_ps[:, :], ALU.mult).then_inc(sems["cm"], 1)
                    lon("vector", v_mul)
                    cnt["cm"] += 1

            assert gix[0] == 36 * 16
            return fns, dict(cnt)

        branches = []
        cnt0 = None
        for p in range(4):
            fns, cnt = plan_attention(p)
            branches.append(fns)
            if cnt0 is None:
                cnt0 = cnt
            assert cnt == cnt0, f"branch {p}: {cnt} != {cnt0}"

        def s_hops(s):
            if "noblk3" in skip:
                return
            for cell in range(32):
                W(s, "ct", cell + 1)
                W(s, "cm", cell)
                s.dma_start(cf[64:128, :], ctmp[0:64, :]).then_inc(sems["cd"], 16)
        on("sync", s_hops)

        def mk_branch(engname):
            def fn(e):
                W(e, "pat", 16)
                with e.register(f"rp_{engname}") as rp:
                    e.reg_load(rp, patb[0:1, 0:1])
                    for p in range(4):
                        with e.If_eq(rp, p):
                            for f in branches[p][engname]:
                                f(e)
            return fn

        if "noblk3" not in skip:
            for engname in ("tensor", "scalar", "vector"):
                on(engname, mk_branch(engname))

        flush("blk3")
        psB.close()

        ao_sb = pb.at(0, DC * OWN, "ao")
        x1_sb = pb.at(16384, DC * OWN, "x1")
        h2f = pb.at(43264, DC * OWN, "h2f")
        wob = [pb.at(76032 + i * 2048, DC * 128, f"wob{i}") for i in range(2)]
        xq4 = pb.at(35072, DC * CT, "xq4")
        sq4 = xq4

        psC = ExitStack()
        st1b = psC.enter_context(nc.psum_tensor("st1b", [1, CT], F32))
        st2b = psC.enter_context(nc.psum_tensor("st2b", [1, CT], F32))
        st3b = psC.enter_context(nc.psum_tensor("st3b", [1, CT], F32))
        st4b = psC.enter_context(nc.psum_tensor("st4b", [1, CT], F32))
        mbp2 = psC.enter_context(nc.psum_tensor("mbp2", [128, CT], F32))
        rbp2 = psC.enter_context(nc.psum_tensor("rbp2", [128, CT], F32))
        aop = [psC.enter_context(nc.psum_tensor(f"aop{i}", [128, CT], F32))
               for i in range(2)]

        SQ4 = NTILE + 4
        RW4 = NTILE + 2

        def g4(g):
            if "b4cut10" in skip:
                return
            iap = bass.AP(xq_d, 0, [[OWN, 128], [128 * OWN, DC], [1, CT]])
            g.dma_start(xq4.ap(0, [[CT, DC], [1, CT]]), iap).then_inc(sems["xr"], 16)
            for o in range(DC):
                if o >= 2:
                    W(g, "ao", 2 * (o - 1))
                iap = bass.AP(wo_d, 128 * o, [[D, 128], [128 * D, DC], [1, 128]])
                g.dma_start(wob[o % 2].ap(0, [[128, DC], [1, 128]]), iap
                            ).then_inc(sems["wo"], 16)
            if "b4chunks" not in skip:
                W(g, "x1", 1)
                iap = bass.AP(xq_d, CT, [[OWN, 128], [128 * OWN, DC], [1, CT]])
                g.dma_start(xq4.ap(0, [[CT, DC], [1, CT]]), iap
                            ).then_inc(sems["xr"], 16)
        on("gpsimd", g4)

        B4CUT = [f for f in skip if f.startswith("b4cut")]
        B4CUT = int(B4CUT[0][5:]) if B4CUT else 99
        if B4CUT == 10:
            skip = tuple(skip) + ("b4cut10",)
        for o in range(DC):
            for tl in range(2):
                gi = 2 * o + tl

                def te(t, o=o, tl=tl, gi=gi):
                    if "b4cut10" in skip:
                        return
                    W(t, "wo", 16 * (o + 1))
                    if gi >= 2 and (B4CUT in (3, 4, 5, 6, 7, 8)):
                        W(t, "aoe", gi - 1)
                    if gi >= 2 and B4CUT in (2, 3, 4):
                        W(t, "sq", SQ4 + gi - 1)
                    for c in range(DC):
                        mm = t.matmul(
                            aop[gi % 2][:, :], wob[o % 2][:, 128 * c:128 * (c + 1)],
                            ctx_sb[:, OWN * c + CT * tl:OWN * c + CT * (tl + 1)],
                            start=(c == 0), stop=(c == DC - 1))
                    mm.then_inc(sems["ao"], 1)
                on("tensor", te)

                def sc(sc, gi=gi):
                    if B4CUT < 2 or B4CUT in (5, 6, 7, 8, 9):
                        return
                    W(sc, "ao", gi + 1)
                    if B4CUT >= 4:
                        W(sc, "stat", SQ4 + gi)
                    sc.activation(aosq[:, :], aop[gi % 2][:, :], AF.Square
                                  ).then_inc(sems["sq"], 1)
                on("scalar", sc)

                def ve(vv, o=o, tl=tl, gi=gi):
                    if B4CUT < 3 and B4CUT not in (5, 6, 7, 8, 9, 10):
                        return
                    W(vv, "sq", SQ4 + gi + 1)
                    if B4CUT in (9, 10):
                        ncopy = int([f for f in skip if f.startswith("nc")][0][2:])                             if any(f.startswith("nc") for f in skip) else 99
                        if gi < ncopy:
                            vv.tensor_copy(kro[:, :], rtmp[:, :])
                        return
                    if B4CUT != 8:
                        W(vv, "ao", gi + 1)
                    if B4CUT == 8:
                        vv.tensor_copy(kro[:, :], rtmp[:, :]
                                       ).then_inc(sems["aoe"], 1)
                        return
                    if B4CUT == 6:
                        vv.tensor_copy(kro[:, :], aop[gi % 2][:, :]
                                       ).then_inc(sems["aoe"], 1)
                    elif B4CUT == 7:
                        vv.tensor_copy(kro[:, :], rtmp[:, :]
                                       ).then_inc(sems["aoe"], 1)
                    else:
                        vv.tensor_copy(ao_sb[:, OWN * o + CT * tl:OWN * o + CT * (tl + 1)],
                                       aop[gi % 2][:, :]).then_inc(sems["aoe"], 1)
                on("vector", ve)

                def ts(t, o=o, tl=tl, gi=gi):
                    if B4CUT < 4:
                        return
                    W(t, "sq", SQ4 + gi + 1)
                    W(t, "aoe", gi + 1)
                    if gi == 0:
                        W(t, "row1", RW4)
                    sa = st1b if tl == 0 else st3b
                    sq2 = st2b if tl == 0 else st4b
                    t.matmul(sa[0:1, :], ones_c[:, :],
                             ao_sb[:, OWN * o + CT * tl:OWN * o + CT * (tl + 1)],
                             start=(o == 0), stop=(o == DC - 1))
                    t.matmul(sq2[0:1, :], ones_c[:, :], aosq[:, :],
                             start=(o == 0), stop=(o == DC - 1)
                             ).then_inc(sems["stat"], 1)
                on("tensor", ts)

        if "b4chunks" in skip:
            flush("blk4")
            raise _StopBuild()

        def v_g1(vv):
            if "gamma" in skip:
                vv.memset(thr[:, :], 0.1).then_inc(sems["gb"], 1)
                return
            W(vv, "gf", NG)
            vv.memset(grow[0:1, :], 0.0)
            vv.tensor_copy(gcb[:, :], gacc[:, :]).then_inc(sems["ga"], 1)
        on("vector", v_g1)

        def t_g(t):
            if "gamma" in skip:
                return
            W(t, "ga", 1)
            t.matmul(mbp2[0:1, 0:NG], ones_c[:, :], gcb[:, :],
                     start=True, stop=True).then_inc(sems["ga"], 1)
        on("tensor", t_g)

        def v_g2(vv):
            if "gamma" in skip:
                return
            W(vv, "ga", 2)
            for i, (a, b) in enumerate(((0, 48), (48, 96), (96, 140))):
                vv.tensor_reduce(grow[0:1, i:i + 1], mbp2[0:1, a:b], AX.X, ALU.add)
            NW = float(FF * D)
            vv.tensor_copy(kro[:, :], rtmp[:, :])
            vv.tensor_scalar(grow[0:1, 4:7], grow[0:1, 0:3], 0.5 / NW, 0.5e-8,
                             ALU.mult, ALU.add)
            vv.tensor_copy(kro[:, :], rtmp[:, :])
            vv.tensor_single_scalar(grow[0:1, 8:11], grow[0:1, 4:7], -1.0, ALU.mult)
            vv.tensor_copy(kro[:, :], rtmp[:, :])
            vv.tensor_copy(rtmp[:, :], kro[:, :]).then_inc(sems["gb"], 1)
        on("vector", v_g2)

        def s_g(s):
            if "gamma" in skip:
                return
            W(s, "gb", 1)
            s.dma_start(gsc_d[0:1, 0:8], grow[0:1, 4:12]).then_inc(sems["ga"], 16)
            W(s, "ga", 18)
            s.dma_start(thr[:, 0:8], bass.AP(gsc_d, 0, [[0, 128], [1, 8]])
                        ).then_inc(sems["ga"], 16)
        on("sync", s_g)

        for tl in range(2):
            idx = RW4 + tl
            sa = st1b if tl == 0 else st3b
            sq2 = st2b if tl == 0 else st4b

            def v1(vv, sa=sa, sq2=sq2, idx=idx, tl=tl):
                W(vv, "stat", SQ4 + 32)
                if tl == 0:
                    W(vv, "gb", 1)
                W(vv, "bc", idx)
                vv.tensor_single_scalar(rowm[0:1, :], sa[0:1, :], 1.0 / D, ALU.mult)
                vv.tensor_tensor(rowv[0:1, :], rowm[0:1, :], rowm[0:1, :], ALU.mult)
                vv.scalar_tensor_tensor(rowv[0:1, :], sq2[0:1, :], 1.0 / D,
                                        rowv[0:1, :], ALU.mult, ALU.subtract).then_inc(sems["row1"], 1)
            on("vector", v1)

            def s1(sc, idx=idx):
                W(sc, "row1", idx + 1)
                sc.activation(rowv[0:1, :], rowv[0:1, :], AF.Ln, bias=cc1(C_EPS))
                sc.activation(rowr[0:1, :], rowv[0:1, :], AF.Exp, scale=cc1(C_NH)
                              ).then_inc(sems["rowsc"], 1)
            on("scalar", s1)

            def v2(vv, idx=idx):
                W(vv, "rowsc", idx + 1)
                vv.tensor_copy(rowrb[0:1, :], rowr[0:1, :])
                vv.tensor_tensor(rowm[0:1, :], rowm[0:1, :], rowr[0:1, :], ALU.mult)
                vv.tensor_single_scalar(rowm[0:1, :], rowm[0:1, :], -1.0, ALU.mult)
                vv.tensor_copy(rownb[0:1, :], rowm[0:1, :]).then_inc(sems["rowb"], 1)
            on("vector", v2)

            def tb(t, idx=idx, tl=tl):
                W(t, "rowb", idx + 1)
                W(t, "bce", idx)
                if tl == 0:
                    W(t, "gb", 1)
                t.matmul(rbp2[:, :], ones_r[0:1, :], rowrb[0:1, :],
                         start=True, stop=True)
                t.matmul(mbp2[:, :], ones_r[0:1, :], rownb[0:1, :],
                         start=True, stop=True).then_inc(sems["bc"], 1)
            on("tensor", tb)

            def ve(vv, idx=idx):
                W(vv, "bc", idx + 1)
                vv.tensor_copy(rbb[:, :], rbp2[:, :])
                vv.tensor_copy(nbb[:, :], mbp2[:, :]).then_inc(sems["bce"], 1)
            on("vector", ve)

            def vx1(vv, tl=tl, idx=idx):
                W(vv, "bce", idx + 1)
                W(vv, "aoe", 32)
                W(vv, "xr", 16 * (tl + 1))
                a = ao_sb.ap(CT * tl, [[OWN, DC], [1, CT]])
                xv = xq4.ap(0, [[CT, DC], [1, CT]])
                o = x1_sb.ap(CT * tl, [[OWN, DC], [1, CT]])
                vv.tensor_tensor(o, a, bc_ap(rbb, DC), ALU.mult)
                vv.tensor_tensor(o, o, bc_ap(nbb, DC), ALU.add)
                vv.tensor_tensor(o, o, xv, ALU.add).then_inc(sems["x1"], 1)
            on("vector", vx1)

        for tl in range(2):
            idx = RW4 + 2 + tl

            def sc(sc, tl=tl, idx=idx):
                W(sc, "x1", 2)
                W(sc, "stat", SQ4 + 32 + tl)
                sc.activation(sq4[:, 0:DC * CT], x1_sb.ap(CT * tl, [[OWN, DC], [1, CT]]),
                              AF.Square).then_inc(sems["sq"], 1)
            on("scalar", sc)

            def te(t, tl=tl, idx=idx):
                W(t, "sq", SQ4 + 32 + tl + 1)
                W(t, "row1", idx)
                for c in range(DC):
                    t.matmul(st1b[0:1, :], ones_c[:, :],
                             x1_sb[:, OWN * c + CT * tl:OWN * c + CT * (tl + 1)],
                             start=(c == 0), stop=(c == DC - 1))
                for c in range(DC):
                    mm = t.matmul(st2b[0:1, :], ones_c[:, :],
                                  sq4[:, CT * c:CT * (c + 1)],
                                  start=(c == 0), stop=(c == DC - 1))
                mm.then_inc(sems["stat"], 1)
            on("tensor", te)

            def v1(vv, idx=idx, tl=tl):
                W(vv, "stat", SQ4 + 32 + tl + 1)
                W(vv, "bc", idx)
                vv.tensor_single_scalar(rowm[0:1, :], st1b[0:1, :], 1.0 / D, ALU.mult)
                vv.tensor_tensor(rowv[0:1, :], rowm[0:1, :], rowm[0:1, :], ALU.mult)
                vv.scalar_tensor_tensor(rowv[0:1, :], st2b[0:1, :], 1.0 / D,
                                        rowv[0:1, :], ALU.mult, ALU.subtract).then_inc(sems["row1"], 1)
            on("vector", v1)

            def s1(sc, idx=idx):
                W(sc, "row1", idx + 1)
                sc.activation(rowv[0:1, :], rowv[0:1, :], AF.Ln, bias=cc1(C_EPS))
                sc.activation(rowr[0:1, :], rowv[0:1, :], AF.Exp, scale=cc1(C_NH)
                              ).then_inc(sems["rowsc"], 1)
            on("scalar", s1)

            def v2(vv, idx=idx):
                W(vv, "rowsc", idx + 1)
                vv.tensor_copy(rowrb[0:1, :], rowr[0:1, :])
                vv.tensor_copy(rowmb[0:1, :], rowm[0:1, :]).then_inc(sems["rowb"], 1)
            on("vector", v2)

            def tb(t, idx=idx):
                W(t, "rowb", idx + 1)
                W(t, "bce", idx)
                t.matmul(rbp2[:, :], ones_r[0:1, :], rowrb[0:1, :],
                         start=True, stop=True)
                t.matmul(mbp2[:, :], ones_r[0:1, :], rowmb[0:1, :],
                         start=True, stop=True).then_inc(sems["bc"], 1)
            on("tensor", tb)

            def ve(vv, idx=idx):
                W(vv, "bc", idx + 1)
                vv.tensor_copy(rbb[:, :], rbp2[:, :])
                vv.tensor_copy(mbb[:, :], mbp2[:, :]).then_inc(sems["bce"], 1)
            on("vector", ve)

            def vh(vv, tl=tl, idx=idx):
                W(vv, "bce", idx + 1)
                src = x1_sb.ap(CT * tl, [[OWN, DC], [1, CT]])
                dst = h2f.ap(CT * tl, [[OWN, DC], [1, CT]])
                vv.tensor_tensor(dst, src, bc_ap(mbb, DC), ALU.subtract)
                vv.tensor_tensor(dst, dst, bc_ap(rbb, DC), ALU.mult).then_inc(sems["h2"], 1)
            on("vector", vh)

        def s_dbg4(s):
            s.dma_start(dbg_d[:, 0:512], pb.t[:, 59648:59648 + 512]
                        ).then_inc(sems["out"], 16)
            W(s, "ga", 34)
            s.dma_start(dbgf_d[:, 0:8], thr[:, 0:8]).then_inc(sems["out"], 16)
        on("sync", s_dbg4)

        def s_x1spill(s):
            if "spill" in skip:
                return
            W(s, "x1", 2)
            s.dma_start(x1s_d[:, :], x1_sb[:, 0:DC * OWN]).then_inc(sems["xs"], 16)
            s.dma_start(dbg_d[:, 512:1024], x1_sb[:, 0:512]).then_inc(sems["out"], 16)
            s.dma_start(dbg_d[:, 1024:1536], h2f[:, 0:512]).then_inc(sems["out"], 16)
            W(s, "xs", 16)
        on("sync", s_x1spill)

        flush("blk4")
        psC.close()

        hidA = pb.at(0, 16 * OWN, "hidA")
        hidB = pb.at(16384, 16 * OWN, "hidB")
        hidC = pb.at(59648, 12 * OWN, "hidC")

        def hid_sl(c, tl):
            if c < 16:
                return hidA[:, OWN * c + CT * tl:OWN * c + CT * (tl + 1)]
            if c < 32:
                c -= 16
                return hidB[:, OWN * c + CT * tl:OWN * c + CT * (tl + 1)]
            c -= 32
            return hidC[:, OWN * c + CT * tl:OWN * c + CT * (tl + 1)]

        ltb = pb.at(33024, 2048, "ltb")
        w1qb = [pb.at(35072 + i * 2048, D, f"w1qb{i}") for i in range(2)]
        w2qb = [pb.at(39168 + i * 2048, D, f"w2qb{i}") for i in range(2)]
        w1f = pf.at(0, D, "w1f")
        w2f = pf.at(2048, D, "w2f")

        psD = ExitStack()
        x1ps = [psD.enter_context(nc.psum_tensor(f"x1ps{i}", [128, CT], F32))
                for i in range(4)]
        x2ps = [psD.enter_context(nc.psum_tensor(f"x2ps{i}", [128, CT], F32))
                for i in range(4)]

        for j in range(FFC):
            slot = j % 2

            def s5(s, j=j):
                if j >= 1:
                    W(s, "w12q", 2 * j - 1)
                iap = bass.AP(w1t_d, 128 * j, [[FF, 128], [128 * FF, DC], [1, 128]])
                s.dma_start(w1f.ap(0, [[128, DC], [1, 128]]), iap
                            ).then_inc(sems["w12"], 16)
                if j >= 1:
                    W(s, "w12q", 2 * j)
                iap = bass.AP(w2t_d, 128 * j, [[FF, 128], [128 * FF, DC], [1, 128]])
                s.dma_start(w2f.ap(0, [[128, DC], [1, 128]]), iap
                            ).then_inc(sems["w12"], 16)
            on("sync", s5)

            def vq(vv, j=j, slot=slot):
                W(vv, "w12", 32 * j + 16)
                if j == 0 and "gamma" not in skip:
                    W(vv, "ga", 34)
                if j >= 2:
                    W(vv, "ffm", 2 * (j - 1))
                vv.tensor_single_scalar(w1qb[slot][:, 0:D], w1f[:, 0:D],
                                        thr[:, 0:1], ALU.is_gt)
                vv.tensor_single_scalar(ltb[:, 0:D], w1f[:, 0:D],
                                        thr[:, 4:5], ALU.is_lt)
                vv.tensor_tensor(w1qb[slot][:, 0:D], w1qb[slot][:, 0:D],
                                 ltb[:, 0:D], ALU.subtract).then_inc(sems["w12q"], 1)
                W(vv, "w12", 32 * j + 32)
                vv.tensor_single_scalar(w2qb[slot][:, 0:D], w2f[:, 0:D],
                                        thr[:, 1:2], ALU.is_gt)
                vv.tensor_single_scalar(ltb[:, 0:D], w2f[:, 0:D],
                                        thr[:, 5:6], ALU.is_lt)
                vv.tensor_tensor(w2qb[slot][:, 0:D], w2qb[slot][:, 0:D],
                                 ltb[:, 0:D], ALU.subtract).then_inc(sems["w12q"], 1)
            on("vector", vq)

            for tl in range(2):
                gi = 2 * j + tl

                bank = gi % 4

                def t5(t, j=j, tl=tl, slot=slot, gi=gi, bank=bank):
                    W(t, "w12q", 2 * j + 2)
                    if gi == 0:
                        W(t, "h2", 2)
                    if gi >= 4:
                        W(t, "hid", gi - 3)
                    for c in range(DC):
                        t.matmul(x1ps[bank][:, :], w1qb[slot][:, 128 * c:128 * (c + 1)],
                                 h2f[:, OWN * c + CT * tl:OWN * c + CT * (tl + 1)],
                                 start=(c == 0), stop=(c == DC - 1))
                    for c in range(DC):
                        mm = t.matmul(x2ps[bank][:, :],
                                      w2qb[slot][:, 128 * c:128 * (c + 1)],
                                      h2f[:, OWN * c + CT * tl:OWN * c + CT * (tl + 1)],
                                      start=(c == 0), stop=(c == DC - 1))
                    mm.then_inc(sems["ffm"], 1)
                on("tensor", t5)

                def s5c(sc, tl=tl, gi=gi, bank=bank):
                    W(sc, "ffm", gi + 1)
                    if gi >= 2:
                        W(sc, "hid", gi - 1)
                    sc.activation(sigb[:, CT * (gi % 2):CT * (gi % 2 + 1)],
                                  x1ps[bank][:, :], AF.Sigmoid
                                  ).then_inc(sems["sig"], 1)
                on("scalar", s5c)

                def v5(vv, j=j, tl=tl, gi=gi, bank=bank):
                    W(vv, "sig", gi + 1)
                    vv.scalar_tensor_tensor(hid_sl(j, tl), x2ps[bank][:, :], 1.0,
                                            sigb[:, CT * (gi % 2):CT * (gi % 2 + 1)],
                                            ALU.mult, ALU.mult).then_inc(sems["hid"], 1)
                on("vector", v5)

        flush("blk5a")
        psD.close()

        w3qb = pb.at(35072, FFC * 128, "w3qb")
        NQ3 = 4
        QW = 11 * 128
        w3f = [pf.at(i * QW, QW, f"w3f{i}") for i in range(2)]

        psE2 = ExitStack()
        fst = [psE2.enter_context(nc.psum_tensor(f"fst{i}", [1, CT], F32))
               for i in range(4)]
        psE1 = ExitStack()
        fps = [psE1.enter_context(nc.psum_tensor(f"fps{i}", [128, CT], F32))
               for i in range(2)]

        def s_dbg5(s):
            W(s, "hid", 2 * FFC)
            s.dma_start(dbg_d[:, 1536:2048], pb.t[:, 0:512]).then_inc(sems["out"], 16)
        on("sync", s_dbg5)

        for o in range(DC):
            for hh in range(NQ3):
                hi = NQ3 * o + hh

                def s5w(s, o=o, hh=hh, hi=hi):
                    if hi >= 2:
                        W(s, "w3q", hi - 1)
                    iap = bass.AP(w3t_d, (QW * hh) * D + 128 * o,
                                  [[D, 128], [128 * D, 11], [1, 128]])
                    s.dma_start(w3f[hh % 2].ap(0, [[128, 11], [1, 128]]), iap
                                ).then_inc(sems["w3"], 16)
                on("sync", s5w)

                def vq3(vv, o=o, hh=hh, hi=hi):
                    W(vv, "w3", 16 * (hi + 1))
                    if hh == 0 and o >= 1:
                        W(vv, "w3m", 2 * o)
                    vv.tensor_single_scalar(w3qb[:, QW * hh:QW * (hh + 1)],
                                            w3f[hh % 2][:, 0:QW],
                                            thr[:, 2:3], ALU.is_gt)
                    vv.tensor_single_scalar(ltb[:, 0:QW], w3f[hh % 2][:, 0:QW],
                                            thr[:, 6:7], ALU.is_lt)
                    vv.tensor_tensor(w3qb[:, QW * hh:QW * (hh + 1)],
                                     w3qb[:, QW * hh:QW * (hh + 1)],
                                     ltb[:, 0:QW], ALU.subtract).then_inc(sems["w3q"], 1)
                on("vector", vq3)

            for tl in range(2):
                gi = 2 * o + tl

                def t3(t, o=o, tl=tl, gi=gi):
                    W(t, "w3q", NQ3 * o + NQ3)
                    if gi == 0:
                        W(t, "hid", 2 * FFC)
                    if gi >= 2:
                        W(t, "fe", gi - 1)
                        W(t, "sig", 2 * FFC + gi - 1)
                    for c in range(FFC):
                        mm = t.matmul(fps[tl][:, :], w3qb[:, 128 * c:128 * (c + 1)],
                                      hid_sl(c, tl),
                                      start=(c == 0), stop=(c == FFC - 1))
                    mm.then_inc(sems["w3m"], 1)
                on("tensor", t3)

                def s3(sc, tl=tl, gi=gi):
                    W(sc, "w3m", gi + 1)
                    if gi >= 2:
                        W(sc, "fst", gi - 1)
                    sc.activation(fsq[:, CT * tl:CT * (tl + 1)], fps[tl][:, :],
                                  AF.Square).then_inc(sems["sig"], 1)
                on("scalar", s3)

                def v3(vv, o=o, tl=tl, gi=gi):
                    W(vv, "w3m", gi + 1)
                    W(vv, "sig", 2 * FFC + gi + 1)
                    W(vv, "h2", 2)
                    vv.tensor_copy(h2f[:, OWN * o + CT * tl:OWN * o + CT * (tl + 1)],
                                   fps[tl][:, :]).then_inc(sems["fe"], 1)
                on("vector", v3)

                def t3s(t, o=o, tl=tl, gi=gi):
                    W(t, "sig", 2 * FFC + gi + 1)
                    W(t, "fe", gi + 1)
                    sa = fst[2 * tl]
                    sq2 = fst[2 * tl + 1]
                    t.matmul(sa[0:1, :], ones_c[:, :],
                             h2f[:, OWN * o + CT * tl:OWN * o + CT * (tl + 1)],
                             start=(o == 0), stop=(o == DC - 1))
                    t.matmul(sq2[0:1, :], ones_c[:, :], fsq[:, CT * tl:CT * (tl + 1)],
                             start=(o == 0), stop=(o == DC - 1)
                             ).then_inc(sems["fst"], 1)
                on("tensor", t3s)

        flush("blk5b")
        psE1.close()

        outb = [pf.at(i * 2048, 4 * CT, f"outb{i}") for i in range(2)]
        x1c = pb.at(0, DC * OWN, "x1c")

        def s_x1load(s):
            s.dma_start(x1c[:, 0:DC * OWN], x1s_d[:, :]).then_inc(sems["xs"], 16)
        on("sync", s_x1load)

        psF = ExitStack()
        mbp3 = psF.enter_context(nc.psum_tensor("mbp3", [128, CT], F32))
        rbp3 = psF.enter_context(nc.psum_tensor("rbp3", [128, CT], F32))

        RW5 = RW4 + 4

        for tl in range(2):
            idx = RW5 + tl
            sa = fst[2 * tl]
            sq2 = fst[2 * tl + 1]

            def v1(vv, sa=sa, sq2=sq2, idx=idx):
                W(vv, "fst", 32)
                W(vv, "bc", idx)
                vv.tensor_single_scalar(rowm[0:1, :], sa[0:1, :], 1.0 / D, ALU.mult)
                vv.tensor_tensor(rowv[0:1, :], rowm[0:1, :], rowm[0:1, :], ALU.mult)
                vv.scalar_tensor_tensor(rowv[0:1, :], sq2[0:1, :], 1.0 / D,
                                        rowv[0:1, :], ALU.mult, ALU.subtract).then_inc(sems["row1"], 1)
            on("vector", v1)

            def s1(sc, idx=idx):
                W(sc, "row1", idx + 1)
                sc.activation(rowv[0:1, :], rowv[0:1, :], AF.Ln, bias=cc1(C_EPS))
                sc.activation(rowr[0:1, :], rowv[0:1, :], AF.Exp, scale=cc1(C_NH)
                              ).then_inc(sems["rowsc"], 1)
            on("scalar", s1)

            def v2(vv, idx=idx):
                W(vv, "rowsc", idx + 1)
                vv.tensor_copy(rowrb[0:1, :], rowr[0:1, :])
                vv.tensor_tensor(rowm[0:1, :], rowm[0:1, :], rowr[0:1, :], ALU.mult)
                vv.tensor_single_scalar(rowm[0:1, :], rowm[0:1, :], -1.0, ALU.mult)
                vv.tensor_copy(rownb[0:1, :], rowm[0:1, :]).then_inc(sems["rowb"], 1)
            on("vector", v2)

            def tb(t, idx=idx):
                W(t, "rowb", idx + 1)
                W(t, "bce", idx)
                t.matmul(rbp3[:, :], ones_r[0:1, :], rowrb[0:1, :],
                         start=True, stop=True)
                t.matmul(mbp3[:, :], ones_r[0:1, :], rownb[0:1, :],
                         start=True, stop=True).then_inc(sems["bc"], 1)
            on("tensor", tb)

            def ve(vv, idx=idx):
                W(vv, "bc", idx + 1)
                vv.tensor_copy(rbb[:, :], rbp3[:, :])
                vv.tensor_copy(nbb[:, :], mbp3[:, :]).then_inc(sems["bce"], 1)
            on("vector", ve)

            for grp in range(4):
                gg = 4 * tl + grp

                def vo(vv, tl=tl, grp=grp, gg=gg, idx=idx):
                    W(vv, "bce", idx + 1)
                    W(vv, "xs", 32)
                    if gg >= 2:
                        W(vv, "out", 16 * (5 + gg - 1))
                    f = h2f.ap(CT * tl + OWN * 4 * grp, [[OWN, 4], [1, CT]])
                    xv = x1c.ap(CT * tl + OWN * 4 * grp, [[OWN, 4], [1, CT]])
                    ob = outb[gg % 2].ap(0, [[CT, 4], [1, CT]])
                    vv.tensor_tensor(ob, f, bc_ap(rbb, 4), ALU.mult)
                    vv.tensor_tensor(ob, ob, bc_ap(nbb, 4), ALU.add)
                    vv.tensor_tensor(ob, ob, xv, ALU.add).then_inc(sems["fin"], 1)
                on("vector", vo)

                def so(s, tl=tl, grp=grp, gg=gg):
                    W(s, "fin", gg + 1)
                    oap = bass.AP(out_d, (512 * grp) * OWN + CT * tl,
                                  [[OWN, 128], [128 * OWN, 4], [1, CT]])
                    s.dma_start(oap, outb[gg % 2].ap(0, [[CT, 4], [1, CT]])
                                ).then_inc(sems["out"], 16)
                on("sync", so)

        def s_fin(s):
            W(s, "out", 16 * 13)
        on("sync", s_fin)

        flush("blk5c")
        psF.close()
        psE2.close()
    except _StopBuild:
        pass

    nc.finalize()
    return nc


def build_program_safe(stop_after=None):
    try:
        return build_program(stop_after)
    except _StopBuild:
        raise


def kernel(**inputs):
    in_maps = _host_prep(**inputs)
    if "nc" not in _cache:
        _cache["nc"] = build_program()
    res = run_bass_kernel_spmd(_cache["nc"], in_maps, core_ids=list(range(8)))
    out_fm = np.zeros((B, D, T), np.float32)
    for c in range(8):
        b, j = c // 4, c % 4
        o = np.asarray(res.results[c]["out"], np.float32)
        out_fm[b][:, CT * j:CT * (j + 1)] = o[:, 0:CT]
        out_fm[b][:, CT * (7 - j):CT * (8 - j)] = o[:, CT:2 * CT]
    return np.ascontiguousarray(out_fm.transpose(0, 2, 1))
